# version 27
# baseline (speedup 1.0000x reference)
import numpy as np
import ml_dtypes
from contextlib import ExitStack

import concourse.bass as bass
import concourse.mybir as mybir
from concourse.bass_utils import run_bass_kernel_spmd

F32 = mybir.dt.float32
BF16 = mybir.dt.bfloat16
I32 = mybir.dt.int32
AF = mybir.ActivationFunctionType
ALU = mybir.AluOpType

N = 4096
D = 1024
SUP = 512
NSUP = N // SUP
T4 = SUP // 128
CH = D // 128
E3 = 3 * D
MAGIC = 0x5F3759DF
import os
SELF_SYNC = set(os.environ.get("KSELF", "rc"))
EPS = 1e-5


class Tok:
    __slots__ = ("sem", "val")

    def __init__(self, sem, val):
        self.sem = sem
        self.val = val


class Plan:
    def __init__(self):
        self.items = []
        self.counts = {}

    def op(self, engine, fn, waits=(), inc=None, amt=1):
        self.items.append((engine, fn, [w for w in waits if w is not None], inc, amt))
        if inc is not None:
            self.counts[inc] = self.counts.get(inc, 0) + amt
            return Tok(inc, self.counts[inc])
        return None


def build_program():
    nc = bass.Bass("TRN2", target_bir_lowering=False, detect_race_conditions=False)

    x_d = nc.dram_tensor("x", [N, D], F32, kind="ExternalInput")
    wq_d = nc.dram_tensor("wq", [E3, D], F32, kind="ExternalInput")
    wf_d = nc.dram_tensor("wf", [D, D], F32, kind="ExternalInput")
    dis_d = nc.dram_tensor("disha", [64, D], F32, kind="ExternalInput")
    gc_d = nc.dram_tensor("gcol_d", [128, CH], F32, kind="ExternalInput")
    idf_d = nc.dram_tensor("idf_d", [128, 128], F32, kind="ExternalInput")
    idb_d = nc.dram_tensor("idb_d", [128, 128], BF16, kind="ExternalInput")
    out_d = nc.dram_tensor("out", [N, D], F32, kind="ExternalOutput")

    with ExitStack() as es:
        def sb(name, sh, dt=F32):
            return es.enter_context(nc.sbuf_tensor(name, sh, dt))

        def ps(name, sh, dt=F32):
            return es.enter_context(nc.psum_tensor(name, sh, dt))

        wgT = sb("wgT", [128, CH * E3], BF16)
        weffT = sb("weffT", [128, CH * D], BF16)
        x_st = sb("x_st", [128, 2 * T4 * D])
        hb = sb("hb", [128, T4 * D], BF16)
        hT = sb("hT", [128, CH * SUP], BF16)
        a_sb = sb("a_sb", [128, CH * SUP], BF16)
        y_sb = sb("y_sb", [128, CH * SUP])
        x2_sb = sb("x2_sb", [128, T4 * D])
        ao_pm = sb("ao_pm", [128, T4 * D])
        u_sb = sb("u_sb", [128, T4 * D], BF16)
        uT = sb("uT", [128, CH * SUP], BF16)
        t4buf = sb("t4buf", [128, 2 * D])
        scr_s = sb("scr_s", [128, D], BF16)
        scr_v = sb("scr_v", [128, D], BF16)
        dis2 = sb("dis2", [128, D])
        gcol = sb("gcol", [128, CH])
        idf = sb("idf", [128, 128])
        idb = sb("idb", [128, 128], BF16)
        stats = sb("stats", [128, 104])
        vv = sb("vv", [128, 4])
        nt1 = sb("nt1", [128, 4])
        nt2 = sb("nt2", [128, 4])
        sbits = sb("sbits", [128, 4], I32)
        carry = sb("carry", [128, CH])

        ps_mm = [ps(f"ps_mm{i}", [128, 512]) for i in range(4)]
        ps_bt = [ps(f"ps_bt{i}", [128, 1024], BF16) for i in range(2)]
        ps_ff = ps("ps_ff", [128, 1024])

        sems = {}
        for nm in ["dc", "dw0", "dw1", "dx0", "dx1", "o", "o2", "V", "S", "T", "P"]:
            sems[nm] = es.enter_context(nc.semaphore("sem_" + nm))
        block = es.enter_context(nc.Block())

        plan = Plan()

        def vop(fn, waits=(), ss=None):
            if "V" in SELF_SYNC or (ss is not None and ss in SELF_SYNC):
                prev = plan.counts.get("V", 0)
                waits = list(waits) + ([Tok("V", prev)] if prev else [])
            return plan.op("V", fn, waits, "V")

        def sop(fn, waits=()):
            if "S" in SELF_SYNC:
                prev = plan.counts.get("S", 0)
                waits = list(waits) + ([Tok("S", prev)] if prev else [])
            return plan.op("S", fn, waits, "S")

        def pop_(fn, waits=()):
            if "P" in SELF_SYNC:
                prev = plan.counts.get("P", 0)
                waits = list(waits) + ([Tok("P", prev)] if prev else [])
            return plan.op("P", fn, waits, "P")

        def top(fn, waits=(), inc="T"):
            return plan.op("T", fn, waits, inc)

        def dma(fn, waits=(), sem="dc"):
            return plan.op("SY", fn, waits, sem, 16)

        def dma_out(fn, waits=(), sem="o"):
            return plan.op("SY", fn, waits, sem, 16)

        dma(lambda e: e.dma_start(gcol[:, :], gc_d[:, :]))
        dma(lambda e: e.dma_start(idf[:, :], idf_d[:, :]))
        dma(lambda e: e.dma_start(idb[:, :], idb_d[:, :]))
        dma(lambda e: e.dma_start(dis2[0:64, :], dis_d[:, :]))
        t_consts = dma(lambda e: e.dma_start(dis2[64:128, :], dis_d[:, :]))

        tok_memset = vop(lambda e: e.memset(carry[:, :], 0.0))
        sop(lambda e: e.preload_act_table(AF.Sigmoid))

        tok_grp = [None] * 8
        tok_cast = [None] * 32
        tok_tr = [None] * 32
        tok_evac = [None] * 32

        ORDER = list(range(8, 24)) + list(range(0, 8)) + list(range(24, 32))
        wready = {}

        def w_src(i):
            ec = ORDER[i]
            if ec < 24:
                return wq_d, ec * 128
            return wf_d, (ec - 24) * 128

        def plan_prep():
            for g in range(8):
                stg = y_sb if g % 2 == 0 else x2_sb
                src0 = w_src(g * 4)
                waits = []
                if g >= 2:
                    waits.append(tok_cast[(g - 2) * 4 + 3])

                for j in range(4):
                    def mk_dma(g=g, stg=stg, j=j):
                        src_d, row0 = w_src(g * 4 + j)
                        def f(e):
                            return e.dma_start(
                                bass.AP(stg, j * D, [[T4 * D, 128], [1, D]]),
                                bass.AP(src_d, row0 * D, [[D, 128], [1, D]]),
                            )
                        return f
                    tok_grp[g] = dma(mk_dma(), waits if j == 0 else [],
                                     sem="dw%d" % (g % 2))

                for j in range(4):
                    i = g * 4 + j
                    slot = i % 4
                    waits = [tok_grp[g]]
                    if i >= 4:
                        waits.append(tok_tr[i - 4])

                    def mk_cast(stg=stg, j=j, slot=slot):
                        def f(e):
                            return e.activation(
                                u_sb[:, slot * D:(slot + 1) * D],
                                stg[:, j * D:(j + 1) * D],
                                AF.Copy)
                        return f
                    tok_cast[i] = sop(mk_cast(), waits)

                    waits = [tok_cast[i], t_consts]
                    if i >= 2:
                        waits.append(tok_evac[i - 2])

                    def mk_tr(slot=slot, b=i % 2):
                        def f(e):
                            for dc in range(CH):
                                ins = e.transpose(
                                    ps_bt[b][:, dc * 128:(dc + 1) * 128],
                                    u_sb[:, slot * D + dc * 128: slot * D + (dc + 1) * 128],
                                    idb[:, :],
                                )
                            return ins
                        return f
                    tok_tr[i] = top(mk_tr(), waits)

                    if ORDER[i] < 24:
                        ec = ORDER[i]

                        def mk_ev(ec=ec, b=i % 2):
                            def f(e):
                                return e.tensor_copy(
                                    bass.AP(wgT, ec * 128,
                                            [[CH * E3, 128], [E3, CH], [1, 128]]),
                                    bass.AP(ps_bt[b], 0,
                                            [[1024, 128], [128, CH], [1, 128]]),
                                )
                            return f
                        tok_evac[i] = vop(mk_ev(), [tok_tr[i]])
                    else:
                        oc = ORDER[i] - 24
                        tk = None
                        for dc in range(CH):
                            def mk_add(oc=oc, dc=dc, b=i % 2):
                                def f(e):
                                    return e.tensor_tensor(
                                        weffT[:, dc * D + oc * 128: dc * D + (oc + 1) * 128],
                                        ps_bt[b][:, dc * 128:(dc + 1) * 128],
                                        dis2[:, oc * 128:(oc + 1) * 128],
                                        op=ALU.add,
                                    )
                                return f
                            tk = vop(mk_add(), [tok_tr[i]] if dc == 0 else [])
                        tok_evac[i] = tk

            wready["q"] = tok_evac[23]
            wready["f"] = tok_evac[31]
            bt_free[0] = tok_evac[30]
            bt_free[1] = tok_evac[31]

        LB = {1: 0, 2: 20, 3: 40, 4: 60}
        NS = NSUP
        tok_xdma = [None] * (NS + 2)
        tok_out = [None] * NS
        tok_outt = [[None] * T4 for _ in range(NS)]
        tok_t2 = [[None] * T4 for _ in range(NS)]
        tok_kv = [[None] * CH for _ in range(NS)]
        tok_a = [[None] * CH for _ in range(NS)]
        tok_sig = [[None] * CH for _ in range(NS)]
        tok_scan = [[None] * CH for _ in range(NS)]
        tok_stt = [[None] * CH for _ in range(NS)]
        tok_q = [[None] * CH for _ in range(NS)]
        tok_hb_T = [None] * (NS + 1)
        tok_apply1 = [[None] * T4 for _ in range(NS + 1)]
        tok_gam = [None] * (NS + 1)
        tok_aoT = [[None] * T4 for _ in range(NS)]
        tok_aoevac = [[None] * T4 for _ in range(NS)]
        tok_x2 = [[None] * T4 for _ in range(NS)]
        tok_x3 = [[None] * T4 for _ in range(NS)]
        tok_u = [[None] * T4 for _ in range(NS)]
        tok_uT_T = [[None] * T4 for _ in range(NS)]
        tok_uevac = [[None] * T4 for _ in range(NS)]
        tok_ffn = [[None] * T4 for _ in range(NS)]
        tok_fevac = [[None] * T4 for _ in range(NS)]
        tok_t4v = [[None] * T4 for _ in range(NS)]
        tok_sq1 = [Tok("S", None) for _ in range(NS + 1)]
        tok_carry = [None] * NS
        tok_sq3 = [None] * NS
        tok_sln3 = [None] * NS
        bt_free = [None, None]

        def row_math(ln, eps, waits, sum_set=0, sum_col=None):
            b = LB[ln]
            if ln == 4:
                sum_c, sq_c, negm_c, rs_c = b + 8, b + 12, b + 16, b + 20
            else:
                sum_c, sq_c, negm_c, rs_c = b + 4 * sum_set, b + 8, b + 12, b + 16
            if sum_col is not None:
                sum_c = sum_col
            vop(lambda e: e.tensor_scalar(
                stats[:, negm_c:negm_c + 4], stats[:, sum_c:sum_c + 4],
                -1.0 / D, None, op0=ALU.mult), waits, ss="r")
            vop(lambda e: e.tensor_tensor(
                nt1[:, :], stats[:, negm_c:negm_c + 4],
                stats[:, negm_c:negm_c + 4], op=ALU.mult), ss="r")
            vop(lambda e: e.tensor_scalar(
                nt2[:, :], stats[:, sq_c:sq_c + 4], 1.0 / D, float(eps),
                op0=ALU.mult, op1=ALU.add), ss="r")
            vop(lambda e: e.tensor_tensor(
                vv[:, :], nt2[:, :], nt1[:, :], op=ALU.subtract), ss="r")
            vop(lambda e: e.tensor_scalar(
                sbits[:, :], vv[:, :].bitcast(I32), 1, None,
                op0=ALU.arith_shift_right), ss="r")
            vop(lambda e: e.tensor_scalar(
                sbits[:, :], sbits[:, :], -1, float(MAGIC),
                op0=ALU.mult, op1=ALU.add), ss="r")
            vop(lambda e: e.tensor_scalar(
                nt1[:, :], vv[:, :], -0.5, None, op0=ALU.mult), ss="r")
            r = sbits[:, :].bitcast(F32)
            for it in range(2):
                vop(lambda e: e.tensor_tensor(nt2[:, :], r, r, op=ALU.mult), ss="r")
                vop(lambda e: e.tensor_tensor(
                    nt2[:, :], nt2[:, :], nt1[:, :], op=ALU.mult), ss="r")
                vop(lambda e: e.tensor_scalar(
                    nt2[:, :], nt2[:, :], 1.0, 1.5, op0=ALU.mult, op1=ALU.add), ss="r")
                dst = r if it == 0 else stats[:, rs_c:rs_c + 4]
                last = vop(lambda e, dst=dst: e.tensor_tensor(
                    dst, r, nt2[:, :], op=ALU.mult), ss="r")
            return last

        def x_dma(s):
            half = s % 2
            waits = []
            if s >= 2:
                waits.append(tok_t4v[s - 2][T4 - 1])

            def f(e):
                return e.dma_start(
                    bass.AP(x_st, half * 4096, [[2 * T4 * D, 128], [D, T4], [1, D]]),
                    bass.AP(x_d, s * SUP * D, [[D, 128], [128 * D, T4], [1, D]]),
                )
            return dma(f, waits, sem="dx%d" % half)

        def pl_cp_sq1(s):
            b = LB[1] + 4 * (s % 2)
            xoff = (s % 2) * 4096
            last = None
            for t in range(T4):
                def mk_cp(t=t, xoff=xoff, b=b):
                    def f(e):
                        return e.activation(scr_s[:, :],
                                            x_st[:, xoff + t * D: xoff + (t + 1) * D],
                                            AF.Copy, accum_out=stats[:, b + t:b + t + 1])
                    return f
                sop(mk_cp(), [tok_xdma[s]])

                def mk_sq(t=t, xoff=xoff, b0=LB[1]):
                    def f(e):
                        return e.activation(scr_s[:, :],
                                            x_st[:, xoff + t * D: xoff + (t + 1) * D],
                                            AF.Square, accum_out=stats[:, b0 + 8 + t:b0 + 9 + t])
                    return f
                last = sop(mk_sq())
            tok_sq1[s].val = last.val

        def pl_apply1(s):
            b = LB[1]
            xoff = (s % 2) * 4096
            rm = row_math(1, 0.0, [tok_sq1[s]], sum_set=s % 2)
            for t in range(T4):
                w = [rm]
                if s >= 1:
                    w.append(tok_hb_T[s - 1])

                def mk_ap(t=t, xoff=xoff, b=b):
                    def f(e):
                        return e.tensor_scalar(
                            hb[:, t * D:(t + 1) * D],
                            x_st[:, xoff + t * D: xoff + (t + 1) * D],
                            stats[:, b + 12 + t:b + 13 + t],
                            stats[:, b + 16 + t:b + 17 + t],
                            op0=ALU.add, op1=ALU.mult)
                    return f
                tok_apply1[s][t] = vop(mk_ap(), w)

        def pl_hT(s):
            last = None
            for t in range(T4):
                bk = t % 2
                w = [tok_apply1[s][t], bt_free[bk]]

                def mk_tr(t=t, bk=bk):
                    def f(e):
                        for dc in range(CH):
                            ins = e.transpose(
                                ps_bt[bk][:, dc * 128:(dc + 1) * 128],
                                hb[:, t * D + dc * 128: t * D + (dc + 1) * 128],
                                idb[:, :])
                        return ins
                    return f
                tk_tr = top(mk_tr(), w)
                tok_hb_T[s] = tk_tr
                lastd = None
                for dc in range(CH):
                    w2 = [tk_tr, t_consts] if dc == 0 else []

                    def mk_ev(t=t, dc=dc, bk=bk):
                        def f(e):
                            return e.activation(
                                hT[:, dc * SUP + t * 128: dc * SUP + (t + 1) * 128],
                                ps_bt[bk][:, dc * 128:(dc + 1) * 128],
                                AF.Copy, scale=gcol[:, dc:dc + 1])
                        return f
                    lastd = sop(mk_ev(), w2)
                bt_free[bk] = lastd
                last = lastd
            tok_gam[s] = last

        def pl_kva(s, c):
            kv_b = c % 2
            a_b = 2 + c % 2
            if c < 2:
                if s >= 1:
                    w_kv = [tok_stt[s - 1][4 + c], tok_gam[s], wready["q"]]
                    w_a = [tok_stt[s - 1][6 + c]]
                else:
                    w_kv = [tok_gam[s], tok_evac[c]]
                    w_a = [tok_evac[8 + c]]
            else:
                w_kv = [tok_scan[s][c - 2], tok_gam[s],
                        tok_evac[c] if s == 0 else wready["q"]]
                w_a = [tok_sig[s][c - 2]] + ([tok_evac[8 + c]] if s == 0 else [])

            def mk_kv(c=c, kv_b=kv_b):
                def f(e):
                    for k in range(CH):
                        ins = e.matmul(
                            ps_mm[kv_b][:, :],
                            wgT[:, k * E3 + (CH + c) * 128: k * E3 + (CH + c + 1) * 128],
                            hT[:, k * SUP:(k + 1) * SUP],
                            start=(k == 0), stop=(k == CH - 1))
                    return ins
                return f
            tok_kv[s][c] = top(mk_kv(), w_kv)

            def mk_a(c=c, a_b=a_b):
                def f(e):
                    for k in range(CH):
                        ins = e.matmul(
                            ps_mm[a_b][:, :],
                            wgT[:, k * E3 + (2 * CH + c) * 128: k * E3 + (2 * CH + c + 1) * 128],
                            hT[:, k * SUP:(k + 1) * SUP],
                            start=(k == 0), stop=(k == CH - 1))
                    return ins
                return f
            tok_a[s][c] = top(mk_a(), w_a)

        def pl_sig(s, c):
            a_b = 2 + c % 2
            w = [tok_a[s][c]]
            if s >= 1:
                w.append(tok_aoT[s - 1][T4 - 1])

            def mk_sig(c=c, a_b=a_b):
                def f(e):
                    return e.activation(a_sb[:, c * SUP:(c + 1) * SUP],
                                        ps_mm[a_b][:, :], AF.Sigmoid)
                return f
            tok_sig[s][c] = sop(mk_sig(), w)

        def pl_scan(s, c):
            kv_b = c % 2
            w = [tok_kv[s][c], tok_sig[s][c],
                 tok_carry[s - 1] if s >= 1 else tok_memset]
            if s == 0:
                w.append(tok_cast[27])

            def mk_scan(c=c, kv_b=kv_b):
                def f(e):
                    return e.tensor_tensor_scan(
                        y_sb[:, c * SUP:(c + 1) * SUP],
                        a_sb[:, c * SUP:(c + 1) * SUP],
                        ps_mm[kv_b][:, :],
                        carry[:, c:c + 1],
                        op0=ALU.mult, op1=ALU.add)
                return f
            tok_scan[s][c] = vop(mk_scan(), w)
            if c == CH - 1:
                tok_carry[s] = vop(lambda e: e.tensor_copy(
                    bass.AP(carry, 0, [[CH, 128], [1, CH]]),
                    bass.AP(y_sb, SUP - 1, [[CH * SUP, 128], [SUP, CH]])), ss="c")

        def pl_q_stt(s, c):
            q_b = c % 4
            if c == 0:
                w = [tok_scan[s][6]]
            elif c == 1:
                w = [tok_scan[s][7]]
            elif c < 4:
                w = [tok_sig[s][4 + c]]
            else:
                w = [tok_stt[s][c - 4]]
            if s == 0:
                w.append(tok_evac[16 + c])

            def mk_q(c=c, q_b=q_b):
                def f(e):
                    for k in range(CH):
                        ins = e.matmul(
                            ps_mm[q_b][:, :],
                            wgT[:, k * E3 + c * 128: k * E3 + (c + 1) * 128],
                            hT[:, k * SUP:(k + 1) * SUP],
                            start=(k == 0), stop=(k == CH - 1))
                    return ins
                return f
            tok_q[s][c] = top(mk_q(), w)

            w = [tok_q[s][c], tok_scan[s][c]]
            if s >= 1:
                w.append(tok_aoT[s - 1][T4 - 1])

            def mk_stt(c=c, q_b=q_b):
                def f(e):
                    return e.scalar_tensor_tensor(
                        a_sb[:, c * SUP:(c + 1) * SUP],
                        y_sb[:, c * SUP:(c + 1) * SUP],
                        1.0, ps_mm[q_b][:, :],
                        op0=ALU.mult, op1=ALU.mult)
                return f
            tok_stt[s][c] = vop(mk_stt(), w)

        tok_xdma[0] = x_dma(0)
        tok_xdma[1] = x_dma(1)
        pl_cp_sq1(0)
        pl_apply1(0)
        pl_hT(0)
        plan_prep()
        for c in range(2):
            pl_kva(0, c)
            pl_sig(0, c)
            pl_scan(0, c)

        for s in range(NSUP):
            xoff = (s % 2) * 4096
            last_s = s == NSUP - 1

            for c in range(CH):
                if c >= 2:
                    pl_kva(s, c)
                    pl_sig(s, c)
                    pl_scan(s, c)
            if not last_s:
                pl_cp_sq1(s + 1)
            for c in range(CH):
                pl_q_stt(s, c)

            if not last_s:
                pl_apply1(s + 1)

            tok_sq2 = None
            tr_early = [None, None]
            for dc in range(CH):
                for t in range(2):
                    w = [tok_stt[s][dc]]
                    if dc == 0:
                        w.append(bt_free[t])

                    def mk_tr1(t=t, dc=dc):
                        def f(e):
                            return e.transpose(
                                ps_bt[t][:, dc * 128:(dc + 1) * 128],
                                a_sb[:, dc * SUP + t * 128: dc * SUP + (t + 1) * 128],
                                idb[:, :])
                        return f
                    tr_early[t] = top(mk_tr1(), w)
            for t in range(T4):
                bk = t % 2
                if t < 2:
                    tok_aoT[s][t] = tr_early[t]
                else:
                    w = [tok_stt[s][CH - 1], bt_free[bk]]

                    def mk_aot(t=t, bk=bk):
                        def f(e):
                            for dc in range(CH):
                                ins = e.transpose(
                                    ps_bt[bk][:, dc * 128:(dc + 1) * 128],
                                    a_sb[:, dc * SUP + t * 128: dc * SUP + (t + 1) * 128],
                                    idb[:, :])
                            return ins
                        return f
                    tok_aoT[s][t] = top(mk_aot(), w)

                b2 = LB[2]
                w = [tok_aoT[s][t]]
                if s >= 1:
                    w.append(tok_t2[s - 1][t])

                def mk_ev(t=t, bk=bk, b2=b2):
                    def f(e):
                        return e.activation(ao_pm[:, t * D:(t + 1) * D],
                                            ps_bt[bk][:, :], AF.Copy,
                                            accum_out=stats[:, b2 + t:b2 + t + 1])
                    return f
                tok_aoevac[s][t] = sop(mk_ev(), w)
                bt_free[bk] = tok_aoevac[s][t]

                def mk_sq2(t=t, b2=b2):
                    def f(e):
                        return e.activation(scr_s[:, :],
                                            ao_pm[:, t * D:(t + 1) * D],
                                            AF.Square,
                                            accum_out=stats[:, b2 + 8 + t:b2 + 9 + t])
                    return f
                tok_sq2 = sop(mk_sq2(), [tok_aoevac[s][t]])

            rm2 = row_math(2, EPS, [tok_sq2])
            b2 = LB[2]
            for t in range(T4):
                w = [rm2]
                if t >= 2:
                    w.append(tok_x2[s][t - 2])
                elif s >= 1:
                    w.append(tok_outt[s - 1][t + 2])

                def mk_t2(t=t, b2=b2):
                    def f(e):
                        return e.tensor_scalar(
                            t4buf[:, (t % 2) * D:(t % 2 + 1) * D],
                            ao_pm[:, t * D:(t + 1) * D],
                            stats[:, b2 + 12 + t:b2 + 13 + t],
                            stats[:, b2 + 16 + t:b2 + 17 + t],
                            op0=ALU.add, op1=ALU.mult)
                    return f
                tk_t2 = vop(mk_t2(), w)
                tok_t2[s][t] = tk_t2

                w = [tk_t2]
                if s >= 1:
                    w.append(tok_u[s - 1][T4 - 1])
                    w.append(tok_sln3[s - 1])

                def mk_x2(t=t, xoff=xoff):
                    def f(e):
                        return e.tensor_tensor(
                            x2_sb[:, t * D:(t + 1) * D],
                            x_st[:, xoff + t * D: xoff + (t + 1) * D],
                            t4buf[:, (t % 2) * D:(t % 2 + 1) * D],
                            op=ALU.add)
                    return f
                tok_x2[s][t] = vop(mk_x2(), w)

                def mk_sq3v(t=t, b3=LB[3]):
                    def f(e):
                        return e.scalar_tensor_tensor(
                            scr_v[:, :], x2_sb[:, t * D:(t + 1) * D], 1.0,
                            x2_sb[:, t * D:(t + 1) * D],
                            op0=ALU.mult, op1=ALU.mult,
                            accum_out=stats[:, b3 + 8 + t:b3 + 9 + t])
                    return f
                tok_sln3[s] = vop(mk_sq3v())

            if not last_s:
                pl_hT(s + 1)
                for c in range(2):
                    pl_kva(s + 1, c)
                    pl_sig(s + 1, c)

            b3 = LB[3]
            rm3 = row_math(3, EPS, [tok_sln3[s]], sum_col=LB[1] + 4 * (s % 2))
            for t in range(T4):
                w = [rm3]
                if s == 0:
                    w.append(tok_tr[28 + t])
                else:
                    w.append(tok_uT_T[s - 1][t])

                def mk_u(t=t, b3=b3):
                    def f(e):
                        return e.tensor_scalar(
                            u_sb[:, t * D:(t + 1) * D],
                            x2_sb[:, t * D:(t + 1) * D],
                            stats[:, b3 + 12 + t:b3 + 13 + t],
                            stats[:, b3 + 16 + t:b3 + 17 + t],
                            op0=ALU.add, op1=ALU.mult)
                    return f
                tok_u[s][t] = vop(mk_u(), w)

            b4 = LB[4]
            tok_sq4 = None
            for t in range(T4):
                bk = t % 2
                w = [tok_u[s][t], bt_free[bk]]

                def mk_tru(t=t, bk=bk):
                    def f(e):
                        for dc in range(CH):
                            ins = e.transpose(
                                ps_bt[bk][:, dc * 128:(dc + 1) * 128],
                                u_sb[:, t * D + dc * 128: t * D + (dc + 1) * 128],
                                idb[:, :])
                        return ins
                    return f
                tok_uT_T[s][t] = top(mk_tru(), w)

                w = [tok_uT_T[s][t]]
                if s >= 1:
                    w.append(tok_ffn[s - 1][t])

                def mk_evu(t=t, bk=bk):
                    def f(e):
                        return e.tensor_copy(
                            bass.AP(uT, t * 128,
                                    [[CH * SUP, 128], [SUP, CH], [1, 128]]),
                            bass.AP(ps_bt[bk], 0,
                                    [[1024, 128], [128, CH], [1, 128]]),
                        )
                    return f
                tok_uevac[s][t] = vop(mk_evu(), w)
                bt_free[bk] = tok_uevac[s][t]

                if not last_s and t < 2:
                    pl_scan(s + 1, t)

                w = [tok_uevac[s][t], wready["f"]]
                if t >= 1:
                    w.append(tok_fevac[s][t - 1])
                elif s >= 1:
                    w.append(tok_fevac[s - 1][T4 - 1])

                def mk_ffn(t=t):
                    def f(e):
                        for k in range(CH):
                            e.matmul(ps_ff[:, 0:512],
                                     uT[:, k * SUP + t * 128: k * SUP + (t + 1) * 128],
                                     weffT[:, k * D: k * D + 512],
                                     start=(k == 0), stop=(k == CH - 1))
                            ins = e.matmul(ps_ff[:, 512:1024],
                                           uT[:, k * SUP + t * 128: k * SUP + (t + 1) * 128],
                                           weffT[:, k * D + 512: (k + 1) * D],
                                           start=(k == 0), stop=(k == CH - 1))
                        return ins
                    return f
                tok_ffn[s][t] = top(mk_ffn(), w)

                w = [tok_ffn[s][t], tok_x2[s][t]]

                def mk_e40(t=t, b4=b4, xoff=xoff):
                    def f(e):
                        return e.activation(x_st[:, xoff + t * D: xoff + t * D + 512],
                                            ps_ff[:, 0:512], AF.Copy,
                                            accum_out=stats[:, b4 + 2 * t:b4 + 2 * t + 1])
                    return f
                sop(mk_e40(), w)

                def mk_e41(t=t, b4=b4, xoff=xoff):
                    def f(e):
                        return e.activation(x_st[:, xoff + t * D + 512: xoff + (t + 1) * D],
                                            ps_ff[:, 512:1024], AF.Copy,
                                            accum_out=stats[:, b4 + 2 * t + 1:b4 + 2 * t + 2])
                    return f
                tok_fevac[s][t] = sop(mk_e41())

                def mk_sq4(t=t, b4=b4, xoff=xoff):
                    def f(e):
                        return e.activation(scr_s[:, :],
                                            x_st[:, xoff + t * D: xoff + (t + 1) * D],
                                            AF.Square,
                                            accum_out=stats[:, b4 + 12 + t:b4 + 13 + t])
                    return f
                tok_sq4 = sop(mk_sq4())

            tok_comb = vop(lambda e: e.tensor_tensor(
                bass.AP(stats, b4 + 8, [[104, 128], [1, 4]]),
                bass.AP(stats, b4, [[104, 128], [2, 4]]),
                bass.AP(stats, b4 + 1, [[104, 128], [2, 4]]),
                op=ALU.add), [tok_sq4])
            rm4 = row_math(4, EPS, [tok_comb])

            for t in range(T4):
                w = [rm4]
                if t >= 2:
                    w.append(tok_outt[s][t - 2])
                else:
                    w.append(tok_x2[s][t + 2])

                def mk_t4(t=t, b4=b4, xoff=xoff):
                    def f(e):
                        return e.tensor_scalar(
                            t4buf[:, (t % 2) * D:(t % 2 + 1) * D],
                            x_st[:, xoff + t * D: xoff + (t + 1) * D],
                            stats[:, b4 + 16 + t:b4 + 17 + t],
                            stats[:, b4 + 20 + t:b4 + 21 + t],
                            op0=ALU.add, op1=ALU.mult)
                    return f
                tok_t4v[s][t] = vop(mk_t4(), w)

                def mk_x3(t=t):
                    def f(e):
                        return e.tensor_tensor(
                            t4buf[:, (t % 2) * D:(t % 2 + 1) * D],
                            x2_sb[:, t * D:(t + 1) * D],
                            t4buf[:, (t % 2) * D:(t % 2 + 1) * D],
                            op=ALU.add)
                    return f
                tok_x3[s][t] = vop(mk_x3(), [tok_t4v[s][t]])

                def mk_outt(s=s, t=t):
                    def f(e):
                        return e.dma_start(
                            bass.AP(out_d, (s * SUP + t * 128) * D, [[D, 128], [1, D]]),
                            bass.AP(t4buf, (t % 2) * D, [[2 * D, 128], [1, D]]),
                        )
                    return f
                tok_outt[s][t] = dma_out(mk_outt(), [tok_x3[s][t]],
                                         sem="o" if t % 2 == 0 else "o2")

            if s + 2 < NSUP:
                tok_xdma[s + 2] = x_dma(s + 2)

        plan.op("SY", None, [Tok("o", 16 * NSUP * 2), Tok("o2", 16 * NSUP * 2)], None)

        def emit(engine_obj, ename):
            waited = {}
            for (en, fn, waits, inc, amt) in plan.items:
                if en != ename:
                    continue
                for tk in waits:
                    assert tk.val is not None, "unresolved future token"
                    if waited.get(tk.sem, 0) >= tk.val:
                        continue
                    engine_obj.wait_ge(sems[tk.sem], tk.val)
                    waited[tk.sem] = tk.val
                if fn is None:
                    continue
                ins = fn(engine_obj)
                if inc is not None:
                    ins.then_inc(sems[inc], amt)

        @block.sync
        def _(sync):
            emit(sync, "SY")

        @block.vector
        def _(vector):
            emit(vector, "V")

        @block.scalar
        def _(scalar):
            emit(scalar, "S")

        @block.tensor
        def _(tensor):
            emit(tensor, "T")

        @block.gpsimd
        def _(gpsimd):
            emit(gpsimd, "P")

    return nc


_NC_CACHE = None


def kernel(L0_attn_x, L0_attn_w_qkva, L0_attn_gamma, L0_ffn_weight, L0_ffn_disha):
    global _NC_CACHE
    x = np.asarray(L0_attn_x, dtype=np.float32)
    wq = np.ascontiguousarray(np.asarray(L0_attn_w_qkva, dtype=np.float32))
    gamma = np.asarray(L0_attn_gamma, dtype=np.float32)
    wf = np.ascontiguousarray(np.asarray(L0_ffn_weight, dtype=np.float32))
    disha = np.ascontiguousarray(np.asarray(L0_ffn_disha, dtype=np.float32))

    gcol = np.ascontiguousarray(gamma.reshape(CH, 128).T)
    idf = np.eye(128, dtype=np.float32)
    idb = np.eye(128, dtype=ml_dtypes.bfloat16)

    if _NC_CACHE is None:
        _NC_CACHE = build_program()
    nc = _NC_CACHE

    in_maps = []
    for b in range(8):
        in_maps.append(dict(
            x=np.ascontiguousarray(x[b]),
            wq=wq, wf=wf, disha=disha, gcol_d=gcol, idf_d=idf, idb_d=idb,
        ))
    res = run_bass_kernel_spmd(nc, in_maps, core_ids=list(range(8)))
    return np.stack([np.asarray(r["out"]) for r in res.results], axis=0)


# version 28 (final)
# speedup vs baseline: 1.0125x; 1.0125x over previous
import numpy as np
import ml_dtypes
from contextlib import ExitStack

import concourse.bass as bass
import concourse.mybir as mybir
from concourse.bass_utils import run_bass_kernel_spmd

F32 = mybir.dt.float32
BF16 = mybir.dt.bfloat16
I32 = mybir.dt.int32
AF = mybir.ActivationFunctionType
ALU = mybir.AluOpType

N = 4096
D = 1024
SUP = 512
NSUP = N // SUP
T4 = SUP // 128
CH = D // 128
E3 = 3 * D
MAGIC = 0x5F3759DF
import os
SELF_SYNC = set(os.environ.get("KSELF", "rc"))
EPS = 1e-5


class Tok:
    __slots__ = ("sem", "val")

    def __init__(self, sem, val):
        self.sem = sem
        self.val = val


class Plan:
    def __init__(self):
        self.items = []
        self.counts = {}

    def op(self, engine, fn, waits=(), inc=None, amt=1):
        self.items.append((engine, fn, [w for w in waits if w is not None], inc, amt))
        if inc is not None:
            self.counts[inc] = self.counts.get(inc, 0) + amt
            return Tok(inc, self.counts[inc])
        return None


def build_program():
    nc = bass.Bass("TRN2", target_bir_lowering=False, detect_race_conditions=False)

    x_d = nc.dram_tensor("x", [N, D], F32, kind="ExternalInput")
    wq_d = nc.dram_tensor("wq", [E3, D], F32, kind="ExternalInput")
    wf_d = nc.dram_tensor("wf", [D, D], F32, kind="ExternalInput")
    dis_d = nc.dram_tensor("disha", [64, D], F32, kind="ExternalInput")
    gc_d = nc.dram_tensor("gcol_d", [128, CH], F32, kind="ExternalInput")
    idf_d = nc.dram_tensor("idf_d", [128, 128], F32, kind="ExternalInput")
    idb_d = nc.dram_tensor("idb_d", [128, 128], BF16, kind="ExternalInput")
    out_d = nc.dram_tensor("out", [N, D], F32, kind="ExternalOutput")

    with ExitStack() as es:
        def sb(name, sh, dt=F32):
            return es.enter_context(nc.sbuf_tensor(name, sh, dt))

        def ps(name, sh, dt=F32):
            return es.enter_context(nc.psum_tensor(name, sh, dt))

        wgT = sb("wgT", [128, CH * E3], BF16)
        weffT = sb("weffT", [128, CH * D], BF16)
        x_st = sb("x_st", [128, 2 * T4 * D])
        hb = sb("hb", [128, T4 * D], BF16)
        hT = sb("hT", [128, CH * SUP], BF16)
        a_sb = sb("a_sb", [128, CH * SUP], BF16)
        y_sb = sb("y_sb", [128, CH * SUP])
        x2_sb = sb("x2_sb", [128, T4 * D])
        ao_pm = sb("ao_pm", [128, T4 * D])
        u_sb = sb("u_sb", [128, T4 * D], BF16)
        uT = sb("uT", [128, CH * SUP], BF16)
        t4buf = sb("t4buf", [128, 2 * D])
        scr_s = sb("scr_s", [128, D], BF16)
        scr_v = sb("scr_v", [128, D], BF16)
        dis2 = sb("dis2", [128, D])
        gcol = sb("gcol", [128, CH])
        idf = sb("idf", [128, 128])
        idb = sb("idb", [128, 128], BF16)
        stats = sb("stats", [128, 104])
        vv = sb("vv", [128, 4])
        nt1 = sb("nt1", [128, 4])
        nt2 = sb("nt2", [128, 4])
        nt3 = sb("nt3", [128, 4])
        sbits = sb("sbits", [128, 4], I32)
        carry = sb("carry", [128, CH])

        ps_mm = [ps(f"ps_mm{i}", [128, 512]) for i in range(4)]
        ps_bt = [ps(f"ps_bt{i}", [128, 1024], BF16) for i in range(2)]
        ps_ff = ps("ps_ff", [128, 1024])

        sems = {}
        for nm in ["dc", "dw0", "dw1", "dx0", "dx1", "o", "o2", "V", "S", "T", "P"]:
            sems[nm] = es.enter_context(nc.semaphore("sem_" + nm))
        block = es.enter_context(nc.Block())

        plan = Plan()

        def vop(fn, waits=(), ss=None):
            if "V" in SELF_SYNC or (ss is not None and ss in SELF_SYNC):
                prev = plan.counts.get("V", 0)
                waits = list(waits) + ([Tok("V", prev)] if prev else [])
            return plan.op("V", fn, waits, "V")

        def sop(fn, waits=()):
            if "S" in SELF_SYNC:
                prev = plan.counts.get("S", 0)
                waits = list(waits) + ([Tok("S", prev)] if prev else [])
            return plan.op("S", fn, waits, "S")

        def pop_(fn, waits=()):
            if "P" in SELF_SYNC:
                prev = plan.counts.get("P", 0)
                waits = list(waits) + ([Tok("P", prev)] if prev else [])
            return plan.op("P", fn, waits, "P")

        def top(fn, waits=(), inc="T"):
            return plan.op("T", fn, waits, inc)

        def dma(fn, waits=(), sem="dc"):
            return plan.op("SY", fn, waits, sem, 16)

        def dma_out(fn, waits=(), sem="o"):
            return plan.op("SY", fn, waits, sem, 16)

        dma(lambda e: e.dma_start(gcol[:, :], gc_d[:, :]))
        dma(lambda e: e.dma_start(idf[:, :], idf_d[:, :]))
        dma(lambda e: e.dma_start(idb[:, :], idb_d[:, :]))
        dma(lambda e: e.dma_start(dis2[0:64, :], dis_d[:, :]))
        t_consts = dma(lambda e: e.dma_start(dis2[64:128, :], dis_d[:, :]))

        tok_memset = vop(lambda e: e.memset(carry[:, :], 0.0))
        sop(lambda e: e.preload_act_table(AF.Sigmoid))

        tok_grp = [None] * 8
        tok_cast = [None] * 32
        tok_tr = [None] * 32
        tok_evac = [None] * 32

        ORDER = list(range(8, 24)) + list(range(0, 8)) + list(range(24, 32))
        wready = {}

        def w_src(i):
            ec = ORDER[i]
            if ec < 24:
                return wq_d, ec * 128
            return wf_d, (ec - 24) * 128

        def plan_prep():
            for g in range(8):
                stg = y_sb if g % 2 == 0 else x2_sb
                src0 = w_src(g * 4)
                waits = []
                if g >= 2:
                    waits.append(tok_cast[(g - 2) * 4 + 3])

                for j in range(4):
                    def mk_dma(g=g, stg=stg, j=j):
                        src_d, row0 = w_src(g * 4 + j)
                        def f(e):
                            return e.dma_start(
                                bass.AP(stg, j * D, [[T4 * D, 128], [1, D]]),
                                bass.AP(src_d, row0 * D, [[D, 128], [1, D]]),
                            )
                        return f
                    tok_grp[g] = dma(mk_dma(), waits if j == 0 else [],
                                     sem="dw%d" % (g % 2))

                for j in range(4):
                    i = g * 4 + j
                    slot = i % 4
                    waits = [tok_grp[g]]
                    if i >= 4:
                        waits.append(tok_tr[i - 4])

                    def mk_cast(stg=stg, j=j, slot=slot):
                        def f(e):
                            return e.activation(
                                u_sb[:, slot * D:(slot + 1) * D],
                                stg[:, j * D:(j + 1) * D],
                                AF.Copy)
                        return f
                    tok_cast[i] = sop(mk_cast(), waits)

                    waits = [tok_cast[i], t_consts]
                    if i >= 2:
                        waits.append(tok_evac[i - 2])

                    def mk_tr(slot=slot, b=i % 2):
                        def f(e):
                            for dc in range(CH):
                                ins = e.transpose(
                                    ps_bt[b][:, dc * 128:(dc + 1) * 128],
                                    u_sb[:, slot * D + dc * 128: slot * D + (dc + 1) * 128],
                                    idb[:, :],
                                )
                            return ins
                        return f
                    tok_tr[i] = top(mk_tr(), waits)

                    if ORDER[i] < 24:
                        ec = ORDER[i]

                        def mk_ev(ec=ec, b=i % 2):
                            def f(e):
                                return e.tensor_copy(
                                    bass.AP(wgT, ec * 128,
                                            [[CH * E3, 128], [E3, CH], [1, 128]]),
                                    bass.AP(ps_bt[b], 0,
                                            [[1024, 128], [128, CH], [1, 128]]),
                                )
                            return f
                        tok_evac[i] = vop(mk_ev(), [tok_tr[i]])
                    else:
                        oc = ORDER[i] - 24
                        tk = None
                        for dc in range(CH):
                            def mk_add(oc=oc, dc=dc, b=i % 2):
                                def f(e):
                                    return e.tensor_tensor(
                                        weffT[:, dc * D + oc * 128: dc * D + (oc + 1) * 128],
                                        ps_bt[b][:, dc * 128:(dc + 1) * 128],
                                        dis2[:, oc * 128:(oc + 1) * 128],
                                        op=ALU.add,
                                    )
                                return f
                            tk = vop(mk_add(), [tok_tr[i]] if dc == 0 else [])
                        tok_evac[i] = tk

            wready["q"] = tok_evac[23]
            wready["f"] = tok_evac[31]
            bt_free[0] = tok_evac[30]
            bt_free[1] = tok_evac[31]

        LB = {1: 0, 2: 20, 3: 40, 4: 60}
        NS = NSUP
        tok_xdma = [None] * (NS + 2)
        tok_out = [None] * NS
        tok_outt = [[None] * T4 for _ in range(NS)]
        tok_t2 = [[None] * T4 for _ in range(NS)]
        tok_kv = [[None] * CH for _ in range(NS)]
        tok_a = [[None] * CH for _ in range(NS)]
        tok_sig = [[None] * CH for _ in range(NS)]
        tok_scan = [[None] * CH for _ in range(NS)]
        tok_stt = [[None] * CH for _ in range(NS)]
        tok_q = [[None] * CH for _ in range(NS)]
        tok_hb_T = [None] * (NS + 1)
        tok_apply1 = [[None] * T4 for _ in range(NS + 1)]
        tok_gam = [None] * (NS + 1)
        tok_aoT = [[None] * T4 for _ in range(NS)]
        tok_aoevac = [[None] * T4 for _ in range(NS)]
        tok_x2 = [[None] * T4 for _ in range(NS)]
        tok_x3 = [[None] * T4 for _ in range(NS)]
        tok_u = [[None] * T4 for _ in range(NS)]
        tok_uT_T = [[None] * T4 for _ in range(NS)]
        tok_uevac = [[None] * T4 for _ in range(NS)]
        tok_ffn = [[None] * T4 for _ in range(NS)]
        tok_fevac = [[None] * T4 for _ in range(NS)]
        tok_t4v = [[None] * T4 for _ in range(NS)]
        tok_sq1 = [Tok("S", None) for _ in range(NS + 1)]
        tok_carry = [None] * NS
        tok_sq3 = [None] * NS
        tok_sln3 = [None] * NS
        bt_free = [None, None]

        def row_math(ln, eps, waits, sum_set=0, sum_col=None, sq_col=None):
            b = LB[ln]
            if ln == 4:
                sum_c, sq_c, negm_c, rs_c = b + 8, b + 12, b + 16, b + 20
            else:
                sum_c, sq_c, negm_c, rs_c = b + 4 * sum_set, b + 8, b + 12, b + 16
            if sum_col is not None:
                sum_c = sum_col
            if sq_col is not None:
                sq_c = sq_col
            vop(lambda e: e.tensor_scalar(
                stats[:, negm_c:negm_c + 4], stats[:, sum_c:sum_c + 4],
                -1.0 / D, None, op0=ALU.mult), waits, ss="r")
            vop(lambda e: e.tensor_tensor(
                nt1[:, :], stats[:, negm_c:negm_c + 4],
                stats[:, negm_c:negm_c + 4], op=ALU.mult), ss="r")
            vop(lambda e: e.tensor_scalar(
                nt2[:, :], stats[:, sq_c:sq_c + 4], 1.0 / D, float(eps),
                op0=ALU.mult, op1=ALU.add), ss="r")
            vop(lambda e: e.tensor_tensor(
                vv[:, :], nt2[:, :], nt1[:, :], op=ALU.subtract), ss="r")
            vop(lambda e: e.tensor_scalar(
                sbits[:, :], vv[:, :].bitcast(I32), 1, None,
                op0=ALU.arith_shift_right), ss="r")
            vop(lambda e: e.tensor_scalar(
                sbits[:, :], sbits[:, :], -1, float(MAGIC),
                op0=ALU.mult, op1=ALU.add), ss="r")
            vop(lambda e: e.tensor_scalar(
                nt1[:, :], vv[:, :], -0.5, None, op0=ALU.mult), ss="r")
            r = sbits[:, :].bitcast(F32)
            for it in range(2):
                vop(lambda e: e.tensor_tensor(nt2[:, :], r, r, op=ALU.mult), ss="r")
                vop(lambda e: e.tensor_tensor(
                    nt2[:, :], nt2[:, :], nt1[:, :], op=ALU.mult), ss="r")
                vop(lambda e: e.tensor_scalar(
                    nt2[:, :], nt2[:, :], 1.0, 1.5, op0=ALU.mult, op1=ALU.add), ss="r")
                dst = r if it == 0 else stats[:, rs_c:rs_c + 4]
                last = vop(lambda e, dst=dst: e.tensor_tensor(
                    dst, r, nt2[:, :], op=ALU.mult), ss="r")
            return last

        def x_dma(s):
            half = s % 2
            waits = []
            if s >= 2:
                waits.append(tok_t4v[s - 2][T4 - 1])

            def f(e):
                return e.dma_start(
                    bass.AP(x_st, half * 4096, [[2 * T4 * D, 128], [D, T4], [1, D]]),
                    bass.AP(x_d, s * SUP * D, [[D, 128], [128 * D, T4], [1, D]]),
                )
            return dma(f, waits, sem="dx%d" % half)

        def pl_cp_sq1(s):
            b = LB[1] + 4 * (s % 2)
            xoff = (s % 2) * 4096
            last = None
            for t in range(T4):
                def mk_cp(t=t, xoff=xoff, b=b):
                    def f(e):
                        return e.activation(scr_s[:, :],
                                            x_st[:, xoff + t * D: xoff + (t + 1) * D],
                                            AF.Copy, accum_out=stats[:, b + t:b + t + 1])
                    return f
                sop(mk_cp(), [tok_xdma[s]])

                def mk_sq(t=t, xoff=xoff, q0=84 + 4 * (s % 2)):
                    def f(e):
                        return e.activation(scr_s[:, :],
                                            x_st[:, xoff + t * D: xoff + (t + 1) * D],
                                            AF.Square, accum_out=stats[:, q0 + t:q0 + t + 1])
                    return f
                last = sop(mk_sq())
            tok_sq1[s].val = last.val

        def pl_apply1(s):
            b = LB[1]
            xoff = (s % 2) * 4096
            rm = row_math(1, 0.0, [tok_sq1[s]], sum_set=s % 2,
                          sq_col=84 + 4 * (s % 2))
            for t in range(T4):
                w = [rm]
                if s >= 1:
                    w.append(tok_hb_T[s - 1])

                def mk_ap(t=t, xoff=xoff, b=b):
                    def f(e):
                        return e.tensor_scalar(
                            hb[:, t * D:(t + 1) * D],
                            x_st[:, xoff + t * D: xoff + (t + 1) * D],
                            stats[:, b + 12 + t:b + 13 + t],
                            stats[:, b + 16 + t:b + 17 + t],
                            op0=ALU.add, op1=ALU.mult)
                    return f
                tok_apply1[s][t] = vop(mk_ap(), w)

        def pl_hT(s):
            last = None
            for t in range(T4):
                bk = t % 2
                w = [tok_apply1[s][t], bt_free[bk]]

                def mk_tr(t=t, bk=bk):
                    def f(e):
                        for dc in range(CH):
                            ins = e.transpose(
                                ps_bt[bk][:, dc * 128:(dc + 1) * 128],
                                hb[:, t * D + dc * 128: t * D + (dc + 1) * 128],
                                idb[:, :])
                        return ins
                    return f
                tk_tr = top(mk_tr(), w)
                tok_hb_T[s] = tk_tr
                lastd = None
                for dc in range(CH):
                    w2 = [tk_tr, t_consts] if dc == 0 else []

                    def mk_ev(t=t, dc=dc, bk=bk):
                        def f(e):
                            return e.activation(
                                hT[:, dc * SUP + t * 128: dc * SUP + (t + 1) * 128],
                                ps_bt[bk][:, dc * 128:(dc + 1) * 128],
                                AF.Copy, scale=gcol[:, dc:dc + 1])
                        return f
                    lastd = sop(mk_ev(), w2)
                bt_free[bk] = lastd
                last = lastd
            tok_gam[s] = last

        def pl_kva(s, c):
            kv_b = c % 2
            a_b = 2 + c % 2
            if c < 2:
                if s >= 1:
                    w_kv = [tok_stt[s - 1][4 + c], tok_gam[s], wready["q"]]
                    w_a = [tok_stt[s - 1][6 + c]]
                else:
                    w_kv = [tok_gam[s], tok_evac[c]]
                    w_a = [tok_evac[8 + c]]
            else:
                w_kv = [tok_scan[s][c - 2], tok_gam[s],
                        tok_evac[c] if s == 0 else wready["q"]]
                w_a = [tok_sig[s][c - 2]] + ([tok_evac[8 + c]] if s == 0 else [])

            def mk_kv(c=c, kv_b=kv_b):
                def f(e):
                    for k in range(CH):
                        ins = e.matmul(
                            ps_mm[kv_b][:, :],
                            wgT[:, k * E3 + (CH + c) * 128: k * E3 + (CH + c + 1) * 128],
                            hT[:, k * SUP:(k + 1) * SUP],
                            start=(k == 0), stop=(k == CH - 1))
                    return ins
                return f
            tok_kv[s][c] = top(mk_kv(), w_kv)

            def mk_a(c=c, a_b=a_b):
                def f(e):
                    for k in range(CH):
                        ins = e.matmul(
                            ps_mm[a_b][:, :],
                            wgT[:, k * E3 + (2 * CH + c) * 128: k * E3 + (2 * CH + c + 1) * 128],
                            hT[:, k * SUP:(k + 1) * SUP],
                            start=(k == 0), stop=(k == CH - 1))
                    return ins
                return f
            tok_a[s][c] = top(mk_a(), w_a)

        def pl_sig(s, c):
            a_b = 2 + c % 2
            w = [tok_a[s][c]]
            if s >= 1:
                w.append(tok_aoT[s - 1][T4 - 1])

            def mk_sig(c=c, a_b=a_b):
                def f(e):
                    return e.activation(a_sb[:, c * SUP:(c + 1) * SUP],
                                        ps_mm[a_b][:, :], AF.Sigmoid)
                return f
            tok_sig[s][c] = sop(mk_sig(), w)

        def pl_scan(s, c):
            kv_b = c % 2
            w = [tok_kv[s][c], tok_sig[s][c],
                 tok_carry[s - 1] if s >= 1 else tok_memset]
            if s == 0:
                w.append(tok_cast[27])

            def mk_scan(c=c, kv_b=kv_b):
                def f(e):
                    return e.tensor_tensor_scan(
                        y_sb[:, c * SUP:(c + 1) * SUP],
                        a_sb[:, c * SUP:(c + 1) * SUP],
                        ps_mm[kv_b][:, :],
                        carry[:, c:c + 1],
                        op0=ALU.mult, op1=ALU.add)
                return f
            tok_scan[s][c] = vop(mk_scan(), w)
            if c == CH - 1:
                tok_carry[s] = vop(lambda e: e.tensor_copy(
                    bass.AP(carry, 0, [[CH, 128], [1, CH]]),
                    bass.AP(y_sb, SUP - 1, [[CH * SUP, 128], [SUP, CH]])), ss="c")

        def pl_q_stt(s, c):
            q_b = c % 4
            if c == 0:
                w = [tok_scan[s][6]]
            elif c == 1:
                w = [tok_scan[s][7]]
            elif c < 4:
                w = [tok_sig[s][4 + c]]
            else:
                w = [tok_stt[s][c - 4]]
            if s == 0:
                w.append(tok_evac[16 + c])

            def mk_q(c=c, q_b=q_b):
                def f(e):
                    for k in range(CH):
                        ins = e.matmul(
                            ps_mm[q_b][:, :],
                            wgT[:, k * E3 + c * 128: k * E3 + (c + 1) * 128],
                            hT[:, k * SUP:(k + 1) * SUP],
                            start=(k == 0), stop=(k == CH - 1))
                    return ins
                return f
            tok_q[s][c] = top(mk_q(), w)

            w = [tok_q[s][c], tok_scan[s][c]]
            if s >= 1:
                w.append(tok_aoT[s - 1][T4 - 1])

            def mk_stt(c=c, q_b=q_b):
                def f(e):
                    return e.scalar_tensor_tensor(
                        a_sb[:, c * SUP:(c + 1) * SUP],
                        y_sb[:, c * SUP:(c + 1) * SUP],
                        1.0, ps_mm[q_b][:, :],
                        op0=ALU.mult, op1=ALU.mult)
                return f
            tok_stt[s][c] = vop(mk_stt(), w)

        tok_xdma[0] = x_dma(0)
        tok_xdma[1] = x_dma(1)
        pl_cp_sq1(0)
        pl_apply1(0)
        pl_hT(0)
        plan_prep()
        for c in range(2):
            pl_kva(0, c)
            pl_sig(0, c)
            pl_scan(0, c)

        for s in range(NSUP):
            xoff = (s % 2) * 4096
            last_s = s == NSUP - 1

            for c in range(CH):
                if c >= 2:
                    pl_kva(s, c)
                    pl_sig(s, c)
                    pl_scan(s, c)
            if not last_s:
                pl_cp_sq1(s + 1)
            for c in range(CH):
                pl_q_stt(s, c)

            if not last_s:
                pl_apply1(s + 1)

            tok_sq2 = None
            tr_early = [None, None]
            for dc in range(CH):
                for t in range(2):
                    w = [tok_stt[s][dc]]
                    if dc == 0:
                        w.append(bt_free[t])

                    def mk_tr1(t=t, dc=dc):
                        def f(e):
                            return e.transpose(
                                ps_bt[t][:, dc * 128:(dc + 1) * 128],
                                a_sb[:, dc * SUP + t * 128: dc * SUP + (t + 1) * 128],
                                idb[:, :])
                        return f
                    tr_early[t] = top(mk_tr1(), w)
            for t in range(T4):
                bk = t % 2
                if t < 2:
                    tok_aoT[s][t] = tr_early[t]
                else:
                    w = [tok_stt[s][CH - 1], bt_free[bk]]

                    def mk_aot(t=t, bk=bk):
                        def f(e):
                            for dc in range(CH):
                                ins = e.transpose(
                                    ps_bt[bk][:, dc * 128:(dc + 1) * 128],
                                    a_sb[:, dc * SUP + t * 128: dc * SUP + (t + 1) * 128],
                                    idb[:, :])
                            return ins
                        return f
                    tok_aoT[s][t] = top(mk_aot(), w)

                b2 = LB[2]
                w = [tok_aoT[s][t]]
                if s >= 1:
                    w.append(tok_t2[s - 1][t])

                def mk_ev(t=t, bk=bk, b2=b2):
                    def f(e):
                        return e.activation(ao_pm[:, t * D:(t + 1) * D],
                                            ps_bt[bk][:, :], AF.Copy,
                                            accum_out=stats[:, b2 + t:b2 + t + 1])
                    return f
                tok_aoevac[s][t] = sop(mk_ev(), w)
                bt_free[bk] = tok_aoevac[s][t]

                def mk_sq2(t=t, b2=b2):
                    def f(e):
                        return e.activation(scr_s[:, :],
                                            ao_pm[:, t * D:(t + 1) * D],
                                            AF.Square,
                                            accum_out=stats[:, b2 + 8 + t:b2 + 9 + t])
                    return f
                tok_sq2 = sop(mk_sq2(), [tok_aoevac[s][t]])

                def mk_dot(t=t, b3=LB[3], xoff=xoff):
                    def f(e):
                        return e.scalar_tensor_tensor(
                            scr_v[:, :], x_st[:, xoff + t * D: xoff + (t + 1) * D],
                            1.0, ao_pm[:, t * D:(t + 1) * D],
                            op0=ALU.mult, op1=ALU.mult,
                            accum_out=stats[:, b3 + 8 + t:b3 + 9 + t])
                    return f
                vop(mk_dot(), [tok_aoevac[s][t]])

            rm2 = row_math(2, EPS, [tok_sq2])
            b2 = LB[2]
            b3 = LB[3]
            sumx_c = LB[1] + 4 * (s % 2)
            q0 = 84 + 4 * (s % 2)
            vop(lambda e: e.tensor_tensor(
                nt1[:, :], stats[:, b2 + 12:b2 + 16],
                stats[:, sumx_c:sumx_c + 4], op=ALU.mult), [rm2], ss="r")
            vop(lambda e: e.tensor_tensor(
                nt1[:, :], stats[:, b3 + 8:b3 + 12], nt1[:, :], op=ALU.add), ss="r")
            vop(lambda e: e.tensor_tensor(
                nt1[:, :], nt1[:, :], stats[:, b2 + 16:b2 + 20], op=ALU.mult), ss="r")
            vop(lambda e: e.tensor_tensor(
                nt2[:, :], stats[:, b2 + 16:b2 + 20],
                stats[:, b2 + 16:b2 + 20], op=ALU.mult), ss="r")
            vop(lambda e: e.tensor_scalar(
                nt3[:, :], vv[:, :], float(D), -float(D) * EPS,
                op0=ALU.mult, op1=ALU.add), ss="r")
            vop(lambda e: e.tensor_tensor(
                nt3[:, :], nt3[:, :], nt2[:, :], op=ALU.mult), ss="r")
            vop(lambda e: e.scalar_tensor_tensor(
                stats[:, b3 + 8:b3 + 12], nt1[:, :], 2.0, nt3[:, :],
                op0=ALU.mult, op1=ALU.add), ss="r")
            vop(lambda e: e.tensor_tensor(
                stats[:, b3 + 8:b3 + 12], stats[:, b3 + 8:b3 + 12],
                stats[:, q0:q0 + 4], op=ALU.add), ss="r")
            rm3 = row_math(3, EPS, [], sum_col=sumx_c)
            for t in range(T4):
                w = [rm2]
                if t >= 2:
                    w.append(tok_x2[s][t - 2])
                elif s >= 1:
                    w.append(tok_outt[s - 1][t + 2])

                def mk_t2(t=t, b2=b2):
                    def f(e):
                        return e.tensor_scalar(
                            t4buf[:, (t % 2) * D:(t % 2 + 1) * D],
                            ao_pm[:, t * D:(t + 1) * D],
                            stats[:, b2 + 12 + t:b2 + 13 + t],
                            stats[:, b2 + 16 + t:b2 + 17 + t],
                            op0=ALU.add, op1=ALU.mult)
                    return f
                tk_t2 = vop(mk_t2(), w)
                tok_t2[s][t] = tk_t2

                w = [tk_t2]
                if s >= 1:
                    w.append(tok_u[s - 1][T4 - 1])

                def mk_x2(t=t, xoff=xoff):
                    def f(e):
                        return e.tensor_tensor(
                            x2_sb[:, t * D:(t + 1) * D],
                            x_st[:, xoff + t * D: xoff + (t + 1) * D],
                            t4buf[:, (t % 2) * D:(t % 2 + 1) * D],
                            op=ALU.add)
                    return f
                tok_x2[s][t] = vop(mk_x2(), w)

                w = [rm3]
                if s == 0:
                    w.append(tok_tr[28 + t])
                else:
                    w.append(tok_uT_T[s - 1][t])

                def mk_u(t=t, b3=b3):
                    def f(e):
                        return e.tensor_scalar(
                            u_sb[:, t * D:(t + 1) * D],
                            x2_sb[:, t * D:(t + 1) * D],
                            stats[:, b3 + 12 + t:b3 + 13 + t],
                            stats[:, b3 + 16 + t:b3 + 17 + t],
                            op0=ALU.add, op1=ALU.mult)
                    return f
                tok_u[s][t] = vop(mk_u(), w)

            if not last_s:
                pl_hT(s + 1)
                for c in range(2):
                    pl_kva(s + 1, c)
                    pl_sig(s + 1, c)

            b4 = LB[4]
            tok_sq4 = None
            for t in range(T4):
                bk = t % 2
                w = [tok_u[s][t], bt_free[bk]]

                def mk_tru(t=t, bk=bk):
                    def f(e):
                        for dc in range(CH):
                            ins = e.transpose(
                                ps_bt[bk][:, dc * 128:(dc + 1) * 128],
                                u_sb[:, t * D + dc * 128: t * D + (dc + 1) * 128],
                                idb[:, :])
                        return ins
                    return f
                tok_uT_T[s][t] = top(mk_tru(), w)

                w = [tok_uT_T[s][t]]
                if s >= 1:
                    w.append(tok_ffn[s - 1][t])

                def mk_evu(t=t, bk=bk):
                    def f(e):
                        return e.tensor_copy(
                            bass.AP(uT, t * 128,
                                    [[CH * SUP, 128], [SUP, CH], [1, 128]]),
                            bass.AP(ps_bt[bk], 0,
                                    [[1024, 128], [128, CH], [1, 128]]),
                        )
                    return f
                tok_uevac[s][t] = vop(mk_evu(), w)
                bt_free[bk] = tok_uevac[s][t]

                if not last_s and t < 2:
                    pl_scan(s + 1, t)

                w = [tok_uevac[s][t], wready["f"]]
                if t >= 1:
                    w.append(tok_fevac[s][t - 1])
                elif s >= 1:
                    w.append(tok_fevac[s - 1][T4 - 1])

                def mk_ffn(t=t):
                    def f(e):
                        for k in range(CH):
                            e.matmul(ps_ff[:, 0:512],
                                     uT[:, k * SUP + t * 128: k * SUP + (t + 1) * 128],
                                     weffT[:, k * D: k * D + 512],
                                     start=(k == 0), stop=(k == CH - 1))
                            ins = e.matmul(ps_ff[:, 512:1024],
                                           uT[:, k * SUP + t * 128: k * SUP + (t + 1) * 128],
                                           weffT[:, k * D + 512: (k + 1) * D],
                                           start=(k == 0), stop=(k == CH - 1))
                        return ins
                    return f
                tok_ffn[s][t] = top(mk_ffn(), w)

                w = [tok_ffn[s][t], tok_x2[s][t]]

                def mk_e40(t=t, b4=b4, xoff=xoff):
                    def f(e):
                        return e.activation(x_st[:, xoff + t * D: xoff + t * D + 512],
                                            ps_ff[:, 0:512], AF.Copy,
                                            accum_out=stats[:, b4 + 2 * t:b4 + 2 * t + 1])
                    return f
                sop(mk_e40(), w)

                def mk_e41(t=t, b4=b4, xoff=xoff):
                    def f(e):
                        return e.activation(x_st[:, xoff + t * D + 512: xoff + (t + 1) * D],
                                            ps_ff[:, 512:1024], AF.Copy,
                                            accum_out=stats[:, b4 + 2 * t + 1:b4 + 2 * t + 2])
                    return f
                tok_fevac[s][t] = sop(mk_e41())

                def mk_sq4(t=t, b4=b4, xoff=xoff):
                    def f(e):
                        return e.activation(scr_s[:, :],
                                            x_st[:, xoff + t * D: xoff + (t + 1) * D],
                                            AF.Square,
                                            accum_out=stats[:, b4 + 12 + t:b4 + 13 + t])
                    return f
                tok_sq4 = sop(mk_sq4())

            tok_comb = vop(lambda e: e.tensor_tensor(
                bass.AP(stats, b4 + 8, [[104, 128], [1, 4]]),
                bass.AP(stats, b4, [[104, 128], [2, 4]]),
                bass.AP(stats, b4 + 1, [[104, 128], [2, 4]]),
                op=ALU.add), [tok_sq4])
            rm4 = row_math(4, EPS, [tok_comb])

            for t in range(T4):
                w = [rm4]
                if t >= 2:
                    w.append(tok_outt[s][t - 2])
                else:
                    w.append(tok_x2[s][t + 2])

                def mk_t4(t=t, b4=b4, xoff=xoff):
                    def f(e):
                        return e.tensor_scalar(
                            t4buf[:, (t % 2) * D:(t % 2 + 1) * D],
                            x_st[:, xoff + t * D: xoff + (t + 1) * D],
                            stats[:, b4 + 16 + t:b4 + 17 + t],
                            stats[:, b4 + 20 + t:b4 + 21 + t],
                            op0=ALU.add, op1=ALU.mult)
                    return f
                tok_t4v[s][t] = vop(mk_t4(), w)

                def mk_x3(t=t):
                    def f(e):
                        return e.tensor_tensor(
                            t4buf[:, (t % 2) * D:(t % 2 + 1) * D],
                            x2_sb[:, t * D:(t + 1) * D],
                            t4buf[:, (t % 2) * D:(t % 2 + 1) * D],
                            op=ALU.add)
                    return f
                tok_x3[s][t] = vop(mk_x3(), [tok_t4v[s][t]])

                def mk_outt(s=s, t=t):
                    def f(e):
                        return e.dma_start(
                            bass.AP(out_d, (s * SUP + t * 128) * D, [[D, 128], [1, D]]),
                            bass.AP(t4buf, (t % 2) * D, [[2 * D, 128], [1, D]]),
                        )
                    return f
                tok_outt[s][t] = dma_out(mk_outt(), [tok_x3[s][t]],
                                         sem="o" if t % 2 == 0 else "o2")

            if s + 2 < NSUP:
                tok_xdma[s + 2] = x_dma(s + 2)

        plan.op("SY", None, [Tok("o", 16 * NSUP * 2), Tok("o2", 16 * NSUP * 2)], None)

        def emit(engine_obj, ename):
            waited = {}
            for (en, fn, waits, inc, amt) in plan.items:
                if en != ename:
                    continue
                for tk in waits:
                    assert tk.val is not None, "unresolved future token"
                    if waited.get(tk.sem, 0) >= tk.val:
                        continue
                    engine_obj.wait_ge(sems[tk.sem], tk.val)
                    waited[tk.sem] = tk.val
                if fn is None:
                    continue
                ins = fn(engine_obj)
                if inc is not None:
                    ins.then_inc(sems[inc], amt)

        @block.sync
        def _(sync):
            emit(sync, "SY")

        @block.vector
        def _(vector):
            emit(vector, "V")

        @block.scalar
        def _(scalar):
            emit(scalar, "S")

        @block.tensor
        def _(tensor):
            emit(tensor, "T")

        @block.gpsimd
        def _(gpsimd):
            emit(gpsimd, "P")

    return nc


_NC_CACHE = None


def kernel(L0_attn_x, L0_attn_w_qkva, L0_attn_gamma, L0_ffn_weight, L0_ffn_disha):
    global _NC_CACHE
    x = np.asarray(L0_attn_x, dtype=np.float32)
    wq = np.ascontiguousarray(np.asarray(L0_attn_w_qkva, dtype=np.float32))
    gamma = np.asarray(L0_attn_gamma, dtype=np.float32)
    wf = np.ascontiguousarray(np.asarray(L0_ffn_weight, dtype=np.float32))
    disha = np.ascontiguousarray(np.asarray(L0_ffn_disha, dtype=np.float32))

    gcol = np.ascontiguousarray(gamma.reshape(CH, 128).T)
    idf = np.eye(128, dtype=np.float32)
    idb = np.eye(128, dtype=ml_dtypes.bfloat16)

    if _NC_CACHE is None:
        _NC_CACHE = build_program()
    nc = _NC_CACHE

    in_maps = []
    for b in range(8):
        in_maps.append(dict(
            x=np.ascontiguousarray(x[b]),
            wq=wq, wf=wf, disha=disha, gcol_d=gcol, idf_d=idf, idb_d=idb,
        ))
    res = run_bass_kernel_spmd(nc, in_maps, core_ids=list(range(8)))
    return np.stack([np.asarray(r["out"]) for r in res.results], axis=0)
